# version 20
# baseline (speedup 1.0000x reference)
from contextlib import ExitStack

import numpy as np
import ml_dtypes

import concourse.bass as bass
import concourse.mybir as mybir
from concourse.bass_utils import run_bass_kernel_spmd

F32 = mybir.dt.float32
BF16 = mybir.dt.bfloat16
FP8 = mybir.dt.float8e4
Act = mybir.ActivationFunctionType
Alu = mybir.AluOpType
AxX = mybir.AxisListType.X
PM_DR = mybir.MatmulPerfMode.DoubleRow

D = 2048
KD = 16
I = 5632
MI = 44
TF = 1025
TM = 1024
ENGINES = ("sync", "tensor", "vector", "scalar", "gpsimd")

THIRDS_FULL = ((0, 342), (342, 342), (684, 341))
THIRDS_MAIN = ((1, 342), (343, 342), (685, 340))
TCOL = 342


class Sem:
    __slots__ = ("h", "n")

    def __init__(self, h):
        self.h = h
        self.n = 0


class Builder:

    N_DSEM = {"sync": 40, "gpsimd": 20, "scalar": 20}

    def __init__(self, nc, es, strict=False):
        self.nc = nc
        self.strict = strict
        self.streams = {e: [] for e in ENGINES}
        self.dsem = {e: [Sem(es.enter_context(nc.semaphore(f"d_{e}{i}")))
                         for i in range(self.N_DSEM[e])]
                     for e in ("sync", "gpsimd", "scalar")}
        self.drot = {e: 0 for e in ("sync", "gpsimd", "scalar")}
        self.psem = {e: Sem(es.enter_context(nc.semaphore(f"p_{e}"))) for e in ENGINES}

    _DRAIN_ENGS = ("vector", "scalar", "gpsimd")

    @staticmethod
    def _ap_rect(a):
        try:
            t = a.tensor
            rl = int(t.shape[-1])
            off = int(a.offset)
            w = int(a.shape[-1])
            return (t.name, off % rl, off % rl + w)
        except Exception:
            return None

    def _hazard(self, eng, outs, ins):
        recent = self._last_out.get(eng) or []
        if not recent:
            return False
        for (r_outs, r_ins) in recent:
            for a in ins:
                for b in r_outs:
                    if a[0] == b[0] and a[1] < b[2] and b[1] < a[2]:
                        return True
            for a in outs:
                for b in r_outs + r_ins:
                    if a[0] == b[0] and a[1] < b[2] and b[1] < a[2]:
                        return True
        return False

    def emit(self, eng, op, *args, inc=None, **kw):
        if not hasattr(self, "_last_out"):
            self._last_out = {}
        if eng in self._DRAIN_ENGS and op != "dma_start" and args:
            def rects(aps):
                out = []
                for a in aps:
                    if hasattr(a, "tensor"):
                        r = self._ap_rect(a)
                        out.append(r if r else ("?", 0, 1 << 40))
                return out
            outs = rects(args[:1])
            ins = rects(list(args[1:]) + list(kw.values()))
            lst = self._last_out.setdefault(eng, [])
            recent2 = lst[-2:]
            narrow = [e2 for e2 in lst if any(o[2] - o[1] <= 64 for o in e2[0])]
            window = lst if self.strict else (recent2 + narrow)
            self._last_out[eng] = window
            if self._hazard(eng, outs, ins):
                self.streams[eng].append({"op": "drain", "args": (), "kw": {},
                                          "incs": []})
                window = []
            self._last_out[eng] = window + [(outs, ins)]
            if len(self._last_out[eng]) > 48:
                del self._last_out[eng][:-48]
        if inc:
            for s, k in inc:
                s.n += k
        self.streams[eng].append({"op": op, "args": args, "kw": kw,
                                  "incs": list(inc or [])})

    def wait(self, eng, sem, val=None):
        v = sem.n if val is None else val
        if v <= 0:
            return
        self.streams[eng].append({"wait": (sem, v)})

    def dma(self, eng, out, in_, **kw):
        sems = self.dsem[eng]
        sm = sems[self.drot[eng] % len(sems)]
        self.drot[eng] += 1
        self.emit(eng, "dma_start", out=out, in_=in_, inc=[(sm, 16)], **kw)
        return (sm, sm.n)

    def waitt(self, eng, tok):
        if tok is None:
            return
        if isinstance(tok, list):
            for t in tok:
                self.waitt(eng, t)
            return
        sem, v = tok
        self.wait(eng, sem, v)

    def wait_dmas(self, eng, queue):
        for sm in self.dsem[queue]:
            self.wait(eng, sm, sm.n)

    def mark(self, eng):
        for entry in reversed(self.streams[eng]):
            if "op" not in entry:
                continue
            if entry["incs"]:
                s, _ = entry["incs"][-1]
                return (s, s.n)
            p = self.psem[eng]
            p.n += 1
            entry["incs"].append((p, 1))
            return (p, p.n)
        return (self.psem[eng], 0)

    def barrier(self):
        tokens = []
        for e in ENGINES:
            for sm in self.dsem.get(e, []):
                if sm.n > 0:
                    tokens.append((e, sm, sm.n))
            tokens.append((e, *self.mark(e)))
        for e in ENGINES:
            for (src, s, v) in tokens:
                if src != e:
                    self.wait(e, s, v)

    def run(self, eng, e):
        for entry in self.streams[eng]:
            if "wait" in entry:
                sem, v = entry["wait"]
                e.wait_ge(sem.h, v)
            else:
                inst = getattr(e, entry["op"])(*entry["args"], **entry["kw"])
                for s, k in entry["incs"]:
                    inst = inst.then_inc(s.h, k)


def build_program(debug=False, strict=False):
    nc = bass.Bass("TRN2", target_bir_lowering=False,
                   detect_race_conditions=strict)
    es = ExitStack()

    ap = {}

    def inp(name, shape, dt):
        ap[name] = nc.dram_tensor(name, list(shape), dt, kind="ExternalInput").ap()

    inp("xin", (D, TF), F32)
    inp("wkT", (KD, 128, KD * 128), F32)
    inp("wvT", (KD, 128, KD * 128), F32)
    inp("wrT", (KD, 128, KD * 128), F32)
    inp("wgT", (MI, 128, KD * 128), F32)
    inp("wuT", (MI, 128, KD * 128), F32)
    inp("wdT", (KD, 128, MI * 128), F32)
    inp("wg_nat", (I, D), F32)
    inp("wu_nat", (I, D), F32)
    inp("wd_nat", (D, I), F32)
    inp("tdt", (128, KD), F32)
    inp("iota", (1, TM), F32)
    inp("identity", (128, 128), F32)
    inp("id224", (128, 128), F32)
    inp("onesbf", (128, 1), BF16)
    inp("onesf", (128, 128), F32)
    inp("msend", (128, 1), F32)
    inp("mrecv", (128, 1), F32)
    yout = nc.dram_tensor("yout", [D, TM], F32, kind="ExternalOutput").ap()

    qg_c = nc.dram_tensor("qg_cache", [MI, 128, KD * 128], FP8, kind="Internal").ap()
    qu_c = nc.dram_tensor("qu_cache", [MI, 128, KD * 128], FP8, kind="Internal").ap()
    qd_c = nc.dram_tensor("qd_cache", [KD, 128, MI * 128], FP8, kind="Internal").ap()
    wkb_c = nc.dram_tensor("wkb_c", [KD, 128, KD * 128], BF16, kind="Internal").ap()
    wvb_c = nc.dram_tensor("wvb_c", [KD, 128, KD * 128], BF16, kind="Internal").ap()
    wrb_c = nc.dram_tensor("wrb_c", [KD, 128, KD * 128], BF16, kind="Internal").ap()
    cc_in = [nc.dram_tensor(f"cc_in{i}", [128, 4], F32, kind="Internal").ap() for i in range(8)]
    cc_out = [nc.dram_tensor(f"cc_out{i}", [128, 4], F32, kind="Internal").ap() for i in range(8)]
    rowbuf = [nc.dram_tensor(f"rowb{i}", [1, TF], F32, kind="Internal").ap() for i in range(4)]
    RG = [[0, 1], [2, 3], [4, 5], [6, 7]]

    dbg = {}
    if debug:
        for name, shape, dt in (
            ("dbg_qx", (128, KD * TCOL), FP8),
            ("dbg_t", (128, MI * TCOL), BF16),
            ("dbg_x1", (D, TF), F32),
            ("dbg_xa", (D, TF), BF16),
            ("dbg_nd", (128, 2 * TM), F32),
            ("dbg_x2", (D, TF), F32),
        ):
            dbg[name] = nc.dram_tensor(name, list(shape), dt, kind="ExternalOutput").ap()

    def sb(name, shape, dt):
        return es.enter_context(nc.sbuf_tensor(name, list(shape), dt))

    xt = sb("xt", (128, KD * TF), F32)
    sbf = sb("sbf", (128, 16 * TF + 2 * TCOL), BF16)
    sf8 = sb("sf8", (128, KD * TCOL + MI * TCOL), FP8)
    scr = sb("scr", (128, 9330), F32)
    ndr = sb("ndr", (128, 8 * TM), BF16)
    wring = sb("wring", (128, 44 * 128), F32)
    qwr = sb("qwr", (128, 64 * 128), FP8)
    bfr = wring[:].bitcast(BF16)
    dqw = wring[:].bitcast(FP8)
    natw = ndr[:].bitcast(F32)
    upstg = sf8[:].bitcast(F32)
    UPS0 = (KD * TCOL) // 4 + 2
    smalls = sb("smalls", (128, 256), F32)
    idt = sb("idt", (128, 128), F32)
    id224 = sb("id224t", (128, 128), F32)
    onesbf = sb("onesbft", (128, 1), BF16)
    onesf = sb("onesft", (128, 128), F32)

    TMP, CMO = 0, 1025
    BC0, BC1 = 2050, 3075
    GO, UO, SGO, T1O = 4100, 4442, 4784, 5126
    RHO, TAU = 5468, 5810
    ROWS, ROWS2, ROWS3 = 6152, 7208, 8264
    EKO, AO = TMP, CMO
    WREP = BC0
    IOTA_B = BC1
    DECM = GO

    S_WSG, S_WSU, S_WSD = 0, 44, 88
    S_W, S_LOGW, S_TD, S_E1 = 104, 120, 136, 224
    S_MSK_S, S_MSK_R = 152, 153
    S_SST, S_SGT = 160, 176

    _ROWMAP = {0: (6152, 0), 1: (6152, 32), 4: (6152, 64),
               3: (7208, 0), 2: (7208, 32),
               5: (8264, 0), 6: (8264, 32)}

    def row(i):
        base, p = _ROWMAP[i]
        return scr[p:p + 1, base:base + 1025]
    R_MEAN, R_RSTD, R_ZMAX, R_C1, R_C0, R_AUX, R_AUX2 = range(7)

    CVO = 8264 + 1030

    def psum(name):
        return es.enter_context(nc.psum_tensor(name, [128, 512], F32))

    ps_a, ps_b = psum("ps_a"), psum("ps_b")
    ps_c, ps_d = psum("ps_c"), psum("ps_d")
    ps_e, ps_f = psum("ps_e"), psum("ps_f")
    ps_aux = psum("ps_aux")

    B = Builder(nc, es, strict=strict)
    pT, pV, pS, pG = B.psem["tensor"], B.psem["vector"], B.psem["scalar"], B.psem["gpsimd"]

    def xsl(k, c0, w):
        return xt[:, k * TF + c0: k * TF + c0 + w]

    def xasl(k, c0, w):
        return sbf[:, k * TF + c0: k * TF + c0 + w]

    def tbsl(m, w):
        return sbf[:, m * TCOL: m * TCOL + w]

    def qxsl(k, w):
        return sf8[:, k * TCOL: k * TCOL + w]

    def qtsl(m, w):
        return sf8[:, KD * TCOL + m * TCOL: KD * TCOL + m * TCOL + w]

    def scrs(off, w):
        return scr[:, off:off + w]

    pp_jobs = []
    q3 = ("sync", "scalar", "gpsimd")
    for m in range(MI):
        pp_jobs.append((ap["wg_nat"][m * 128:(m + 1) * 128, :], S_WSG + m, 1.0, True))
        pp_jobs.append((ap["wu_nat"][m * 128:(m + 1) * 128, :], S_WSU + m, 1.0, True))
    for j in range(KD):
        for p in range(3):
            pp_jobs.append((ap["wd_nat"][j * 128:(j + 1) * 128, p * 2048:min((p + 1) * 2048, I)],
                            S_WSD + j, 2.0, p == 2))
    pp = {"emitted": 0, "issued": 0, "toks": {}, "red": {}, "acc_first": True,
          "done": {}}

    def _pp_issue(i):
        src = pp_jobs[i][0]
        slot = (i % 2) * 2048
        B.wait(q3[i % 3], pV, pp["red"].get(i - 2, 0))
        pp["toks"][i] = B.dma(q3[i % 3], natw[:, slot:slot + src.shape[1]], src)
        pp["issued"] = i + 1

    def prepass_until(njobs):
        while pp["emitted"] < min(njobs, len(pp_jobs)):
            i = pp["emitted"]
            while pp["issued"] < min(i + 2, len(pp_jobs)):
                _pp_issue(pp["issued"])
            src, ws_col, sc, fin = pp_jobs[i]
            slot = (i % 2) * 2048
            B.waitt("vector", pp["toks"].pop(i))
            B.emit("vector", "tensor_reduce", scr[:, CVO:CVO + 1], natw[:, slot:slot + src.shape[1]],
                   AxX, Alu.max, apply_absolute_value=True)
            if ws_col < S_WSD:
                B.emit("vector", "tensor_scalar_max", scr[:, CVO:CVO + 1], scr[:, CVO:CVO + 1], 1e-30)
                B.emit("vector", "tensor_scalar_mul", smalls[:, ws_col:ws_col + 1], scr[:, CVO:CVO + 1],
                       1.0 / (224.0 * sc), inc=[(pV, 1)])
            else:
                if pp["acc_first"]:
                    B.emit("vector", "tensor_copy", scr[:, CVO + 4:CVO + 5], scr[:, CVO:CVO + 1])
                else:
                    B.emit("vector", "tensor_max", scr[:, CVO + 4:CVO + 5], scr[:, CVO + 4:CVO + 5],
                           scr[:, CVO:CVO + 1])
                pp["acc_first"] = False
                if fin:
                    B.emit("vector", "tensor_scalar_max", scr[:, CVO + 4:CVO + 5], scr[:, CVO + 4:CVO + 5], 1e-30)
                    B.emit("vector", "tensor_scalar_mul", smalls[:, ws_col:ws_col + 1], scr[:, CVO + 4:CVO + 5],
                           1.0 / (224.0 * sc), inc=[(pV, 1)])
                    pp["acc_first"] = True
                else:
                    B.emit("vector", "engine_nop", inc=[(pV, 1)])
            pp["red"][i] = pV.n
            pp["done"][i] = pV.n
            pp["emitted"] = i + 1
            if pp["issued"] < min(i + 3, len(pp_jobs)):
                _pp_issue(pp["issued"])

    _XQ = ("sync", "scalar", "gpsimd")
    xtoks = []
    for k in range(KD):
        xtoks.append(B.dma(_XQ[k % 3], xsl(k, 0, TF), ap["xin"][k * 128:(k + 1) * 128, :]))
    B.dma("sync", smalls[:, S_TD:S_TD + KD], ap["tdt"][:])
    B.dma("sync", smalls[:, S_MSK_S:S_MSK_S + 1], ap["msend"][:])
    B.dma("sync", smalls[:, S_MSK_R:S_MSK_R + 1], ap["mrecv"][:])
    B.dma("sync", idt[:], ap["identity"][:])
    B.dma("sync", id224[:], ap["id224"][:])
    B.dma("sync", onesbf[:], ap["onesbf"][:])
    B.dma("sync", onesf[:], ap["onesf"][:])
    for f in range(KD):
        B.dma("gpsimd", wkb_c[f], ap["wkT"][f])
        B.dma("gpsimd", wvb_c[f], ap["wvT"][f])
        B.dma("gpsimd", wrb_c[f], ap["wrT"][f])
    B.wait_dmas("scalar", "sync")
    B.emit("scalar", "activation", smalls[:, S_E1:S_E1 + KD], smalls[:, S_TD:S_TD + KD], Act.Exp, inc=[(pS, 1)])
    B.wait("vector", pS)
    B.emit("vector", "tensor_scalar_mul", smalls[:, S_LOGW:S_LOGW + KD], smalls[:, S_E1:S_E1 + KD], -1.0, inc=[(pV, 1)])
    B.wait("scalar", pV)
    B.emit("scalar", "activation", smalls[:, S_W:S_W + KD], smalls[:, S_LOGW:S_LOGW + KD], Act.Exp, inc=[(pS, 1)])
    B.barrier()

    def ln_stats(cols):
        c0, w = cols
        sq = [sbf[:, 16 * TF:16 * TF + TCOL], sbf[:, 16 * TF + TCOL:16 * TF + 2 * TCOL]]
        for k in range(KD):
            B.emit("vector", "tensor_copy", xasl(k, c0, w), xsl(k, c0, w), inc=[(pV, 1)])
        vcopy = pV.n
        B.wait("scalar", pV, vcopy)
        ccs = [(c0 + i, min(342, w - i)) for i in range(0, w, 342)]
        evac_done = 0
        for sc0, sw in ccs:
            mm_prev = []
            for k in range(KD):
                s = sq[k % 2]
                if mm_prev:
                    B.wait("scalar", pT, mm_prev[-1] if k >= 2 else 0)
                B.emit("scalar", "activation", s[:, :sw], xasl(k, sc0, sw), Act.Square, inc=[(pS, 1)])
                B.wait("tensor", pS)
                if k == 0:
                    B.wait("tensor", pV, evac_done)
                B.emit("tensor", "matmul", ps_e[0:1, 0:sw], onesbf[:], xasl(k, sc0, sw),
                       start=(k == 0), stop=(k == KD - 1))
                B.emit("tensor", "matmul", ps_f[0:1, 0:sw], onesbf[:], s[:, :sw],
                       start=(k == 0), stop=(k == KD - 1), inc=[(pT, 1)])
                mm_prev.append(pT.n)
            B.wait("vector", pT)
            r0 = sc0 - c0
            mean = row(R_MEAN)[:, r0:r0 + sw]
            rstd = row(R_RSTD)[:, r0:r0 + sw]
            aux = row(R_AUX)[:, r0:r0 + sw]
            aux2 = row(R_AUX2)[:, r0:r0 + sw]
            B.emit("vector", "tensor_scalar_mul", mean, ps_e[0:1, 0:sw], 1.0 / D)
            B.emit("vector", "tensor_mul", aux2, mean, mean)
            B.emit("vector", "scalar_tensor_tensor", aux, ps_f[0:1, 0:sw], 1.0 / D, aux2, Alu.mult, Alu.subtract)
            B.emit("vector", "tensor_scalar_add", aux, aux, 1e-5)
            B.emit("vector", "reciprocal", aux2, aux, inc=[(pV, 1)])
            evac_done = pV.n
            B.wait("scalar", pV, evac_done)
            B.emit("scalar", "activation", rstd, aux2, Act.Sqrt, inc=[(pS, 1)])
        return pS.n

    _rb = [0]

    def bcast(row_ap, dst_off, w):
        i = _rb[0] % 4
        _rb[0] += 1
        d1 = B.dma("sync", rowbuf[i][:, 0:w], row_ap)
        B.waitt("sync", d1)
        B.wait("sync", pV)
        return B.dma("sync", scrs(dst_off, w), rowbuf[i][:, 0:w].broadcast_to([128, w]))

    def colmax_finish(cm_off, cm_ready_pv, w, out_row_i, op=Alu.max, negate=False):
        B.wait("tensor", pV, cm_ready_pv)
        out_r = row(out_row_i)
        red_done = 0
        dma_done = None
        for j in range((w + 127) // 128):
            c = min(128, w - j * 128)
            B.wait("tensor", pV, red_done)
            B.emit("tensor", "transpose", ps_aux[0:c, 0:128], scrs(cm_off + j * 128, c), idt[:], inc=[(pT, 1)])
            B.wait("vector", pT)
            B.waitt("vector", dma_done)
            B.emit("vector", "tensor_reduce", scr[0:c, CVO:CVO + 1], ps_aux[0:c, 0:128], AxX, op,
                   negate=negate, inc=[(pV, 1)])
            red_done = pV.n
            B.wait("sync", pV, red_done)
            dma_done = B.dma("sync", out_r[:, j * 128:j * 128 + c], scr[0:c, CVO:CVO + 1])
        B.waitt("vector", dma_done)
        return pV.n

    def prep_weight_chunk(srcT_chunk, nkt, stage_ap, stage_tok, qoff, ws_col, sc, cache_chunk):
        B.emit("vector", "reciprocal", scr[:, CVO + 6:CVO + 7], smalls[:, ws_col:ws_col + 1])
        B.emit("vector", "tensor_scalar", scrs(T1O, 128), idt[:], scr[:, CVO + 6:CVO + 7],
               1.0 / sc, Alu.mult, Alu.mult, inc=[(pV, 1)])
        B.wait("tensor", pV)
        B.wait("tensor", pS)
        B.emit("tensor", "matmul", ps_aux[:, 256:384], onesf[:], scrs(T1O, 128), start=True, stop=True, inc=[(pT, 1)])
        B.wait("vector", pT)
        B.emit("vector", "tensor_copy", scrs(T1O, 128), ps_aux[:, 256:384], inc=[(pV, 1)])
        B.waitt("vector", stage_tok)
        B.wait_dmas("vector", "gpsimd")
        B.wait("vector", pT)
        rb = scrs(T1O, 128).rearrange("p (o n) -> p o n", o=1).broadcast_to([128, nkt, 128])
        sv = stage_ap[:, 0:nkt * 128].rearrange("p (k n) -> p k n", k=nkt)
        ov = qwr[:, qoff:qoff + nkt * 128].rearrange("p (k n) -> p k n", k=nkt)
        B.emit("vector", "tensor_mul", ov, sv, rb, inc=[(pV, 1)])
        q_done = pV.n
        B.wait("gpsimd", pV, q_done)
        st = B.dma("gpsimd", cache_chunk, qwr[:, qoff:qoff + nkt * 128])
        return q_done, st

    def emit_ffn(call):
        thirds = THIRDS_FULL if call == 1 else THIRDS_MAIN
        c0, w = (0, TF) if call == 1 else (1, TM)
        stats_ps = ln_stats((c0, w))
        for k in range(KD):
            xs_ = xsl(k, c0, w)
            B.emit("vector", "tensor_max", scrs(CMO, w), xs_ if k == 0 else scrs(CMO, w), xs_)
            B.emit("vector", "tensor_tensor", scrs(TMP, w), xs_ if k == 0 else scrs(TMP, w), xs_, Alu.min, inc=[(pV, 1)])
        tree_pv = pV.n
        colmax_finish(CMO, tree_pv, w, R_C1)
        mdone = colmax_finish(TMP, pV.n, w, R_AUX, op=Alu.min, negate=True)
        B.wait("vector", pS, stats_ps)
        B.wait("vector", pV, mdone)
        B.emit("vector", "tensor_sub", row(R_ZMAX)[:, 0:w], row(R_C1)[:, 0:w], row(R_MEAN)[:, 0:w])
        B.emit("vector", "tensor_add", row(R_AUX2)[:, 0:w], row(R_AUX)[:, 0:w], row(R_MEAN)[:, 0:w])
        B.emit("vector", "tensor_max", row(R_ZMAX)[:, 0:w], row(R_ZMAX)[:, 0:w], row(R_AUX2)[:, 0:w])
        B.emit("vector", "tensor_scalar_max", row(R_ZMAX)[:, 0:w], row(R_ZMAX)[:, 0:w], 1e-30, inc=[(pV, 1)])
        B.emit("vector", "reciprocal", row(R_AUX)[:, 0:w], row(R_ZMAX)[:, 0:w])
        B.emit("vector", "tensor_scalar_mul", row(R_C1)[:, 0:w], row(R_RSTD)[:, 0:w], 224.0)
        B.emit("vector", "tensor_mul", row(R_C1)[:, 0:w], row(R_C1)[:, 0:w], row(R_AUX)[:, 0:w])
        B.emit("vector", "scalar_tensor_tensor", row(R_C0)[:, 0:w], row(R_MEAN)[:, 0:w], -1.0, row(R_C1)[:, 0:w], Alu.mult, Alu.mult)
        B.emit("vector", "tensor_scalar_mul", row(R_ZMAX)[:, 0:w], row(R_ZMAX)[:, 0:w], 1.0 / 224.0, inc=[(pV, 1)])
        B.wait("sync", pV)
        b1 = bcast(row(R_C1)[:, 0:w], BC1, w)
        b2 = bcast(row(R_C0)[:, 0:w], BC0, w)

        for ti, (t0, tw) in enumerate(thirds):
            toff = t0 - c0
            prep = (call == 1 and ti == 0)
            B.waitt("vector", [b1, b2])
            for k in range(KD):
                B.emit("vector", "tensor_mul", scrs(TMP, tw), xsl(k, t0, tw), scr[:, BC1 + toff:BC1 + toff + tw])
                B.emit("vector", "tensor_add", qxsl(k, tw), scrs(TMP, tw), scr[:, BC0 + toff:BC0 + toff + tw], inc=[(pV, 1)])
            qx_done = pV.n
            rho_dma = bcast(row(R_ZMAX)[:, toff:toff + tw], RHO, tw)
            if debug and call == 1 and ti == 0:
                B.wait("gpsimd", pV, qx_done)
                for k in range(KD):
                    B.dma("gpsimd", dbg["dbg_qx"][:, k * TCOL:k * TCOL + tw], qxsl(k, tw))

            evac_done = {}
            mm_done = {}
            load_done = {}
            stage_toks = {}

            def stage_gu(m):
                gsl = wring[:, (m % 2) * 2048:(m % 2) * 2048 + 2048]
                usl = upstg[:, UPS0:UPS0 + 2048]
                t1_ = B.dma("sync", gsl, ap["wgT"][m])
                if m > 0:
                    B.wait("scalar", pV)
                t2_ = B.dma("scalar", usl, ap["wuT"][m])
                stage_toks[m] = (gsl, usl, [t1_, t2_])

            def gu_load(m):
                qb = (m % 2) * 32 * 128
                B.wait("scalar", pT, mm_done.get(m - 2, 0))
                t1_ = B.dma("scalar", qwr[:, qb:qb + 16 * 128], qg_c[m])
                t2_ = B.dma("scalar", qwr[:, qb + 16 * 128:qb + 32 * 128], qu_c[m])
                load_done[m] = [t1_, t2_]

            if not prep:
                B.wait_dmas("scalar", "gpsimd")
                B.wait("scalar", pT)
                gu_load(0)
                gu_load(1)
            else:
                stage_gu(0)
            cm_ready = 0
            for m in range(MI):
                pg = ps_a if m % 2 == 0 else ps_b
                pu = ps_c if m % 2 == 0 else ps_d
                qb = (m % 2) * 32 * 128
                if prep:
                    prepass_until(2 * (min(m + 4, MI)))
                    gsl, usl, stoks = stage_toks.pop(m)
                    _q, _st0 = prep_weight_chunk(ap["wgT"][m], KD, gsl, [stoks[0]], qb, S_WSG + m, 1.0, qg_c[m])
                    wready, _st = prep_weight_chunk(ap["wuT"][m], KD, usl, [stoks[1]], qb + 16 * 128,
                                                    S_WSU + m, 1.0, qu_c[m])
                    if m + 1 < MI and (m + 1) not in stage_toks:
                        B.wait("sync", pV, wready)
                        stage_gu(m + 1)
                    elif m + 2 < MI:
                        B.wait("sync", pV, wready)
                        stage_gu(m + 2)
                    B.wait("tensor", pV, max(wready, qx_done))
                else:
                    B.waitt("tensor", load_done[m])
                    B.wait("tensor", pV, qx_done)
                B.wait("tensor", pV, evac_done.get(m - 2, 0))
                for kk in range(0, KD, 2):
                    lhsT = qwr[:, qb + kk * 128:qb + (kk + 2) * 128].rearrange("p (ko d) -> p ko d", ko=2)
                    rhs = sf8[:, kk * TCOL:(kk + 2) * TCOL].rearrange("p (ko n) -> p ko n", ko=2)[:, :, 0:tw]
                    B.emit("tensor", "matmul", pg[:, 0:tw], lhsT, rhs,
                           start=(kk == 0), stop=(kk == KD - 2), perf_mode=PM_DR,
                           inc=[(pT, 1)] if kk == KD - 2 else None)
                for kk in range(0, KD, 2):
                    lhsT = qwr[:, qb + (16 + kk) * 128:qb + (18 + kk) * 128].rearrange("p (ko d) -> p ko d", ko=2)
                    rhs = sf8[:, kk * TCOL:(kk + 2) * TCOL].rearrange("p (ko n) -> p ko n", ko=2)[:, :, 0:tw]
                    B.emit("tensor", "matmul", pu[:, 0:tw], lhsT, rhs,
                           start=(kk == 0), stop=(kk == KD - 2), perf_mode=PM_DR,
                           inc=[(pT, 1)] if kk == KD - 2 else None)
                mm_done[m] = pT.n
                if (not prep) and m + 2 < MI:
                    gu_load(m + 2)
                B.wait("vector", pT, mm_done[m])
                B.waitt("vector", rho_dma)
                B.emit("vector", "scalar_tensor_tensor", scrs(GO, tw), pg[:, 0:tw],
                       smalls[:, S_WSG + m:S_WSG + m + 1], scrs(RHO, tw), Alu.mult, Alu.mult, inc=[(pV, 1)])
                B.wait("scalar", pV, pV.n)
                B.emit("scalar", "activation", scrs(SGO, tw), scrs(GO, tw), Act.Sigmoid, inc=[(pS, 1)])
                B.wait("vector", pS, pS.n)
                B.emit("vector", "tensor_mul", scrs(T1O, tw), scrs(GO, tw), scrs(SGO, tw))
                B.emit("vector", "scalar_tensor_tensor", tbsl(m, tw), pu[:, 0:tw],
                       smalls[:, S_WSU + m:S_WSU + m + 1], scrs(T1O, tw), Alu.mult, Alu.mult, inc=[(pV, 1)])
                evac_done[m] = pV.n
                B.wait("scalar", pV, pV.n)
                B.emit("scalar", "activation", scrs(GO, tw), tbsl(m, tw), Act.Abs, inc=[(pS, 1)])
                B.wait("vector", pS, pS.n)
                if m == 0:
                    B.emit("vector", "tensor_copy", scrs(CMO, tw), scrs(GO, tw), inc=[(pV, 1)])
                else:
                    B.emit("vector", "tensor_max", scrs(CMO, tw), scrs(CMO, tw), scrs(GO, tw), inc=[(pV, 1)])
                cm_ready = pV.n
            tdone = colmax_finish(CMO, cm_ready, tw, R_AUX2)
            B.wait("vector", pV, tdone)
            B.emit("vector", "tensor_scalar_max", row(R_AUX2)[:, 0:tw], row(R_AUX2)[:, 0:tw], 1e-30)
            B.emit("vector", "reciprocal", row(R_AUX)[:, 0:tw], row(R_AUX2)[:, 0:tw])
            B.emit("vector", "tensor_scalar_mul", row(R_C1)[:, 0:tw], row(R_AUX)[:, 0:tw], 224.0)
            B.emit("vector", "tensor_mul", row(R_AUX2)[:, 0:tw], row(R_AUX2)[:, 0:tw],
                   row(R_ZMAX)[:, toff:toff + tw])
            B.emit("vector", "tensor_scalar_mul", row(R_AUX2)[:, 0:tw], row(R_AUX2)[:, 0:tw], 1.0 / 224.0, inc=[(pV, 1)])
            B.wait("sync", pV)
            B.waitt("sync", rho_dma)
            qrho = bcast(row(R_C1)[:, 0:tw], RHO, tw)
            qdma = bcast(row(R_AUX2)[:, 0:tw], TAU, tw)
            B.waitt("vector", [qrho, qdma])
            tb3 = sbf[:, 0:MI * TCOL].rearrange("p (k n) -> p k n", k=MI)[:, :, 0:tw]
            qt3 = sf8[:, KD * TCOL:KD * TCOL + MI * TCOL].rearrange("p (k n) -> p k n", k=MI)[:, :, 0:tw]
            rb3 = scrs(RHO, TCOL).rearrange("p (o n) -> p o n", o=1).broadcast_to([128, MI, TCOL])[:, :, 0:tw]
            B.emit("vector", "tensor_mul", qt3, tb3, rb3, inc=[(pV, 1)])
            qt_done = pV.n
            if debug and call == 1 and ti == 0:
                B.wait("gpsimd", pV, qt_done)
                for m in range(MI):
                    B.dma("gpsimd", dbg["dbg_t"][:, m * TCOL:m * TCOL + tw], tbsl(m, tw))

            if prep:
                prepass_until(len(pp_jobs))
            ev_y = {}
            mmd_done = {}
            dload = {}

            def d_load(j):
                qb = (j % 2) * 44 * 128
                B.wait("scalar", pT, mmd_done.get(j - 2, 0))
                dload[j] = B.dma("scalar", dqw[:, qb:qb + 44 * 128], qd_c[j])

            if not prep:
                B.wait("scalar", pV, qt_done)
                B.wait_dmas("scalar", "gpsimd")
                d_load(0)
                d_load(1)
            for j in range(KD):
                py = ps_e if j % 2 == 0 else ps_f
                if prep:
                    B.wait("sync", pV)
                    B.wait("scalar", pV)
                    dt1 = B.dma("sync", wring[:, 0:22 * 128], ap["wdT"][j][:, 0:22 * 128])
                    dt2 = B.dma("scalar", wring[:, 22 * 128:44 * 128], ap["wdT"][j][:, 22 * 128:44 * 128])
                    wready, _st = prep_weight_chunk(ap["wdT"][j], MI, wring[:, 0:44 * 128],
                                                    [dt1, dt2], 0, S_WSD + j, 2.0, qd_c[j])
                    B.wait("tensor", pV, max(wready, qt_done))
                    src = qwr
                    qb = 0
                else:
                    src = dqw
                    qb = (j % 2) * 44 * 128
                    B.waitt("tensor", dload[j])
                    B.wait("tensor", pV, qt_done)
                B.wait("tensor", pV, ev_y.get(j - 2, 0))
                for mk in range(0, MI, 2):
                    lhsT = src[:, qb + mk * 128:qb + (mk + 2) * 128].rearrange("p (ko d) -> p ko d", ko=2)
                    rhs = sf8[:, KD * TCOL + mk * TCOL:KD * TCOL + (mk + 2) * TCOL].rearrange(
                        "p (ko n) -> p ko n", ko=2)[:, :, 0:tw]
                    B.emit("tensor", "matmul", py[:, 0:tw], lhsT, rhs,
                           start=(mk == 0), stop=(mk == MI - 2), perf_mode=PM_DR,
                           inc=[(pT, 1)] if mk == MI - 2 else None)
                mmd_done[j] = pT.n
                if (not prep) and j + 2 < KD:
                    d_load(j + 2)
                B.wait("vector", pT, mmd_done[j])
                B.emit("vector", "scalar_tensor_tensor", scrs(T1O, tw), py[:, 0:tw],
                       smalls[:, S_WSD + j:S_WSD + j + 1], scrs(TAU, tw), Alu.mult, Alu.mult)
                B.emit("vector", "tensor_add", xsl(j, t0, tw), xsl(j, t0, tw), scrs(T1O, tw), inc=[(pV, 1)])
                ev_y[j] = pV.n
        B.barrier()

    def emit_attn():
        stats_ps = ln_stats((0, TF))
        B.wait("sync", pS, stats_ps)
        b1 = bcast(row(R_MEAN)[:, 0:TF], BC1, TF)
        b2 = bcast(row(R_RSTD)[:, 0:TF], BC0, TF)
        B.waitt("vector", [b1, b2])
        for k in range(KD):
            B.emit("vector", "tensor_sub", scrs(TMP, TF), xsl(k, 0, TF), scrs(BC1, TF))
            B.emit("vector", "tensor_mul", xasl(k, 0, TF), scrs(TMP, TF), scrs(BC0, TF), inc=[(pV, 1)])
        xa_done = pV.n
        if debug:
            B.wait("gpsimd", pV, xa_done)
            for k in range(KD):
                B.dma("gpsimd", dbg["dbg_xa"][k * 128:(k + 1) * 128, :], xasl(k, 0, TF))
        B.wait("sync", pV)
        bio = B.dma("sync", scrs(IOTA_B, TM), ap["iota"].broadcast_to([128, TM]))

        KS = [0, 2048]
        VS = [4096, 6144]
        RS = 8192
        kv_dma = {}
        k_used = {}
        v_used = {}
        r_used = [0]

        def kv_load(f):
            B.wait("sync", pT, k_used.get(f - 2, 0))
            t1_ = B.dma("sync", bfr[:, KS[f % 2]:KS[f % 2] + 2048], wkb_c[f])
            B.wait("sync", pT, v_used.get(f - 2, 0))
            t2_ = B.dma("sync", bfr[:, VS[f % 2]:VS[f % 2] + 2048], wvb_c[f])
            kv_dma[f] = [t1_, t2_]

        ek_done = {}
        amul_done = {}
        scans_done = {}
        scan_slot_free = {}
        cc_done = {}
        stash_done = {}
        ekao_free = [0]

        def attn_chunk(f):
            B.waitt("tensor", kv_dma[f])
            B.wait("tensor", pV, xa_done)
            B.wait("tensor", pS, ek_done.get(f - 1, 0))
            B.wait("tensor", pV, amul_done.get(f - 1, 0))
            base = KS[f % 2]
            for cc in range(2):
                for kk in range(KD):
                    B.emit("tensor", "matmul", (ps_a if cc == 0 else ps_b)[:, 0:512],
                           bfr[:, base + kk * 128:base + (kk + 1) * 128],
                           xasl(kk, cc * 512, 512),
                           start=(kk == 0), stop=(kk == KD - 1),
                           inc=[(pT, 1)] if kk == KD - 1 else None)
            k_done = pT.n
            k_used[f] = k_done
            vbase = VS[f % 2]
            for cc in range(2):
                for kk in range(KD):
                    B.emit("tensor", "matmul", (ps_c if cc == 0 else ps_d)[:, 0:512],
                           bfr[:, vbase + kk * 128:vbase + (kk + 1) * 128],
                           xasl(kk, cc * 512, 512),
                           start=(kk == 0), stop=(kk == KD - 1),
                           inc=[(pT, 1)] if kk == KD - 1 else None)
            v_done = pT.n
            v_used[f] = v_done
            if f + 2 < KD:
                kv_load(f + 2)
            B.wait("scalar", pT, k_done)
            B.wait("scalar", pV, max(scans_done.get(f - 1, 0), ekao_free[0]))
            B.emit("scalar", "activation", scrs(EKO, 512), ps_a[:, 0:512], Act.Exp)
            B.emit("scalar", "activation", scrs(EKO + 512, 512), ps_b[:, 0:512], Act.Exp, inc=[(pS, 1)])
            ek_done[f] = pS.n
            B.wait("vector", pS, ek_done[f])
            B.wait("vector", pT, v_done)
            B.wait("vector", pV, scan_slot_free.get(f - 4, 0))
            B.emit("vector", "tensor_mul", scrs(AO, 512), scrs(EKO, 512), ps_c[:, 0:512])
            B.emit("vector", "tensor_mul", scrs(AO + 512, 512), scrs(EKO + 512, 512), ps_d[:, 0:512], inc=[(pV, 1)])
            amul_done[f] = pV.n
            B.emit("vector", "memset", scrs(WREP, TM), 1.0)
            B.emit("vector", "tensor_scalar_mul", scrs(WREP, TM), scrs(WREP, TM), smalls[:, S_W + f:S_W + f + 1])
            slot = (f % 4) * 2 * TM
            B.emit("vector", "tensor_tensor_scan", ndr[:, slot:slot + TM], scrs(WREP, TM), scrs(AO, TM), 0.0, Alu.mult, Alu.add)
            B.emit("vector", "tensor_tensor_scan", ndr[:, slot + TM:slot + 2 * TM], scrs(WREP, TM), scrs(EKO, TM), 0.0, Alu.mult, Alu.add, inc=[(pV, 1)])
            scans_done[f] = pV.n
            half = f // 2
            sst = S_SST + 8 * (half % 2)
            B.waitt("vector", cc_done.get(half - 2))
            B.emit("vector", "tensor_scalar_mul", smalls[:, sst + 2 * (f % 2):sst + 2 * (f % 2) + 1],
                   ndr[:, slot + TM - 1:slot + TM], smalls[:, S_MSK_S:S_MSK_S + 1])
            B.emit("vector", "tensor_scalar_mul", smalls[:, sst + 2 * (f % 2) + 1:sst + 2 * (f % 2) + 2],
                   ndr[:, slot + 2 * TM - 1:slot + 2 * TM], smalls[:, S_MSK_S:S_MSK_S + 1], inc=[(pV, 1)])
            stash_done[half] = pV.n

        def issue_coll(half):
            sst = S_SST + 8 * (half % 2)
            sgt = S_SGT + 8 * (half % 2)
            B.wait("gpsimd", pV, stash_done[half])
            ci_tok = B.dma("gpsimd", cc_in[half][:, 0:4], smalls[:, sst:sst + 4])
            B.waitt("gpsimd", ci_tok)
            B.emit("gpsimd", "collective_compute", "AllReduce", Alu.add,
                   ins=[cc_in[half][:]], outs=[cc_out[half][:]], replica_groups=RG, inc=[(pG, 1)])
            B.wait("gpsimd", pG)
            cc_done[half] = B.dma("gpsimd", smalls[:, sgt:sgt + 4], cc_out[half][:, 0:4])

        rload = {}

        def finalize(f):
            half = f // 2
            slot = (f % 4) * 2 * TM
            sgt = S_SGT + 8 * (half % 2)
            B.wait("scalar", pT, r_used[0])
            rload[f] = B.dma("scalar", bfr[:, RS:RS + 2048], wrb_c[f])
            B.waitt("tensor", rload[f])
            B.wait("tensor", pV, scan_slot_free.get(f - 1, 0))
            for cc in range(2):
                for kk in range(KD):
                    B.emit("tensor", "matmul", (ps_e if cc == 0 else ps_f)[:, 0:512],
                           bfr[:, RS + kk * 128:RS + (kk + 1) * 128],
                           xasl(kk, 1 + cc * 512, 512),
                           start=(kk == 0), stop=(kk == KD - 1),
                           inc=[(pT, 1)] if kk == KD - 1 else None)
            r_done = pT.n
            r_used[0] = r_done
            B.waitt("vector", cc_done[half])
            B.emit("vector", "tensor_scalar_mul", smalls[:, S_SGT + 16:S_SGT + 17],
                   smalls[:, sgt + 2 * (f % 2):sgt + 2 * (f % 2) + 1], smalls[:, S_MSK_R:S_MSK_R + 1])
            B.emit("vector", "tensor_scalar_mul", smalls[:, S_SGT + 17:S_SGT + 18],
                   smalls[:, sgt + 2 * (f % 2) + 1:sgt + 2 * (f % 2) + 2], smalls[:, S_MSK_R:S_MSK_R + 1], inc=[(pV, 1)])
            corr_done = pV.n
            B.waitt("scalar", bio)
            B.wait("scalar", pV, corr_done)
            B.emit("scalar", "activation", scrs(DECM, TM), scrs(IOTA_B, TM), Act.Exp,
                   scale=smalls[:, S_LOGW + f:S_LOGW + f + 1], inc=[(pS, 1)])
            B.wait("vector", pS, pS.n)
            B.emit("vector", "scalar_tensor_tensor", scrs(EKO, TM), scrs(DECM, TM),
                   smalls[:, S_SGT + 16:S_SGT + 17], ndr[:, slot:slot + TM], Alu.mult, Alu.add)
            B.emit("vector", "scalar_tensor_tensor", scrs(AO, TM), scrs(DECM, TM),
                   smalls[:, S_SGT + 17:S_SGT + 18], ndr[:, slot + TM:slot + 2 * TM], Alu.mult, Alu.add)
            B.emit("vector", "tensor_scalar_add", scrs(AO, TM), scrs(AO, TM), 1e-8)
            B.emit("vector", "reciprocal", scrs(WREP, TM), scrs(AO, TM))
            B.emit("vector", "tensor_mul", scrs(EKO, TM), scrs(EKO, TM), scrs(WREP, TM), inc=[(pV, 1)])
            wkv_done = pV.n
            if debug and f == 1:
                B.wait("gpsimd", pV, wkv_done)
                B.dma("gpsimd", dbg["dbg_nd"][:, 0:TM], scrs(EKO, TM))
                B.dma("gpsimd", dbg["dbg_nd"][:, TM:2 * TM], scrs(AO, TM))
                B.wait_dmas("vector", "gpsimd")
            B.wait("vector", pT, r_done)
            B.emit("vector", "tensor_mul", scrs(AO, 512), ps_e[:, 0:512], scrs(EKO, 512))
            B.emit("vector", "tensor_mul", scrs(AO + 512, 512), ps_f[:, 0:512], scrs(EKO + 512, 512))
            B.emit("vector", "tensor_add", xsl(f, 1, TM), xsl(f, 1, TM), scrs(AO, TM), inc=[(pV, 1)])
            scan_slot_free[f] = pV.n
            ekao_free[0] = pV.n

        kv_load(0)
        kv_load(1)
        for half in range(8):
            attn_chunk(2 * half)
            attn_chunk(2 * half + 1)
            issue_coll(half)
            if half > 0:
                finalize(2 * half - 2)
                finalize(2 * half - 1)
        finalize(14)
        finalize(15)
        B.barrier()

    emit_ffn(1)
    if debug:
        for k in range(KD):
            B.dma("gpsimd", dbg["dbg_x1"][k * 128:(k + 1) * 128, :], xsl(k, 0, TF))
        B.barrier()
    emit_attn()
    if debug:
        for k in range(KD):
            B.dma("gpsimd", dbg["dbg_x2"][k * 128:(k + 1) * 128, :], xsl(k, 0, TF))
        B.barrier()
    emit_ffn(2)

    for k in range(KD):
        B.dma("sync", yout[k * 128:(k + 1) * 128, :], xsl(k, 1, TM))
    B.wait_dmas("sync", "sync")

    with nc.Block() as block:
        @block.sync
        def _(e):
            B.run("sync", e)

        @block.tensor
        def _(e):
            B.run("tensor", e)

        @block.vector
        def _(e):
            B.run("vector", e)

        @block.scalar
        def _(e):
            B.run("scalar", e)

        @block.gpsimd
        def _(e):
            B.run("gpsimd", e)

    es.close()
    return nc


def make_in_maps(inputs):
    x = np.ascontiguousarray(np.asarray(inputs["L0_attn_x"], dtype=np.float32))
    P = np.asarray(inputs["L0_ffn_down_reorder_index"]).astype(np.int64)
    wg = np.asarray(inputs["L0_ffn_w_gate"], dtype=np.float32)
    wu = np.asarray(inputs["L0_ffn_w_up"], dtype=np.float32)
    wd = np.asarray(inputs["L0_ffn_w_down"], dtype=np.float32)
    wk = np.asarray(inputs["L0_attn_Wk"], dtype=np.float32)
    wv = np.asarray(inputs["L0_attn_Wv"], dtype=np.float32)
    wr = np.asarray(inputs["L0_attn_Wr"], dtype=np.float32)
    td = np.asarray(inputs["L0_attn_time_decay"], dtype=np.float32)

    wg_nat = np.ascontiguousarray(wg[P, :])
    wu_nat = np.ascontiguousarray(wu[P, :])

    def til(wmat, nm):
        no, ni = wmat.shape
        a = wmat.reshape(no // 128, 128, ni // 128, 128)
        return np.ascontiguousarray(a.transpose(0, 3, 2, 1).reshape(no // 128, 128, ni))

    wgT = til(wg_nat, "g")
    wuT = til(wu_nat, "u")
    wdT = til(wd, "d")
    wkT = til(wk, "k")
    wvT = til(wv, "v")
    wrT = til(wr, "r")
    tdt = np.ascontiguousarray(td.reshape(KD, 128).T)
    iota = np.arange(1, TM + 1, dtype=np.float32).reshape(1, TM)
    ident = np.eye(128, dtype=np.float32)
    onesbf = np.ones((128, 1), dtype=ml_dtypes.bfloat16)
    onesf = np.ones((128, 128), dtype=np.float32)

    in_maps = []
    for c in range(8):
        b, h = c // 2, c % 2
        if h == 0:
            xs = np.concatenate([np.zeros((1, D), np.float32), x[b, 0:TM]], axis=0)
        else:
            xs = x[b, TM - 1:2 * TM]
        in_maps.append({
            "xin": np.ascontiguousarray(xs.T),
            "wkT": wkT, "wvT": wvT, "wrT": wrT,
            "wgT": wgT, "wuT": wuT, "wdT": wdT, "tdt": tdt,
            "wg_nat": wg_nat, "wu_nat": wu_nat, "wd_nat": wd,
            "iota": iota, "identity": ident, "id224": ident * 224.0,
            "onesbf": onesbf, "onesf": onesf,
            "msend": np.full((128, 1), 1.0 - h, np.float32),
            "mrecv": np.full((128, 1), float(h), np.float32),
        })
    return in_maps


def kernel(**inputs):
    nc = build_program(debug=False)
    in_maps = make_in_maps(inputs)
    res = run_bass_kernel_spmd(nc, in_maps, core_ids=list(range(8)))
    out = np.empty((4, 2048, D), dtype=np.float32)
    for c in range(8):
        b, h = c // 2, c % 2
        out[b, h * TM:(h + 1) * TM, :] = res.results[c]["yout"].T
    return out


# version 21 (final)
# speedup vs baseline: 1.0584x; 1.0584x over previous
from contextlib import ExitStack

import numpy as np
import ml_dtypes

import concourse.bass as bass
import concourse.mybir as mybir
from concourse.bass_utils import run_bass_kernel_spmd

F32 = mybir.dt.float32
BF16 = mybir.dt.bfloat16
FP8 = mybir.dt.float8e4
Act = mybir.ActivationFunctionType
Alu = mybir.AluOpType
AxX = mybir.AxisListType.X
PM_DR = mybir.MatmulPerfMode.DoubleRow

D = 2048
KD = 16
I = 5632
MI = 44
TF = 1025
TM = 1024
ENGINES = ("sync", "tensor", "vector", "scalar", "gpsimd")

THIRDS_FULL = ((0, 342), (342, 342), (684, 341))
THIRDS_MAIN = ((1, 342), (343, 342), (685, 340))
TCOL = 342


class Sem:
    __slots__ = ("h", "n")

    def __init__(self, h):
        self.h = h
        self.n = 0


class Builder:

    N_DSEM = {"sync": 40, "gpsimd": 20, "scalar": 20}

    def __init__(self, nc, es, strict=False):
        self.nc = nc
        self.strict = strict
        self.streams = {e: [] for e in ENGINES}
        self.dsem = {e: [Sem(es.enter_context(nc.semaphore(f"d_{e}{i}")))
                         for i in range(self.N_DSEM[e])]
                     for e in ("sync", "gpsimd", "scalar")}
        self.drot = {e: 0 for e in ("sync", "gpsimd", "scalar")}
        self.psem = {e: Sem(es.enter_context(nc.semaphore(f"p_{e}"))) for e in ENGINES}

    _DRAIN_ENGS = ("vector", "scalar", "gpsimd")

    @staticmethod
    def _ap_rect(a):
        try:
            t = a.tensor
            rl = int(t.shape[-1])
            off = int(a.offset)
            w = int(a.shape[-1])
            return (t.name, off % rl, off % rl + w)
        except Exception:
            return None

    def _hazard(self, eng, outs, ins):
        recent = self._last_out.get(eng) or []
        if not recent:
            return False
        for (r_outs, r_ins) in recent:
            for a in ins:
                for b in r_outs:
                    if a[0] == b[0] and a[1] < b[2] and b[1] < a[2]:
                        return True
            for a in outs:
                for b in r_outs + r_ins:
                    if a[0] == b[0] and a[1] < b[2] and b[1] < a[2]:
                        return True
        return False

    def emit(self, eng, op, *args, inc=None, **kw):
        if not hasattr(self, "_last_out"):
            self._last_out = {}
        if eng in self._DRAIN_ENGS and op != "dma_start" and args:
            def rects(aps):
                out = []
                for a in aps:
                    if hasattr(a, "tensor"):
                        r = self._ap_rect(a)
                        out.append(r if r else ("?", 0, 1 << 40))
                return out
            outs = rects(args[:1])
            ins = rects(list(args[1:]) + list(kw.values()))
            lst = self._last_out.setdefault(eng, [])
            recent2 = lst[-2:]
            narrow = [e2 for e2 in lst if any(o[2] - o[1] <= 64 for o in e2[0])]
            window = lst if self.strict else (recent2 + narrow)
            self._last_out[eng] = window
            if self._hazard(eng, outs, ins):
                self.streams[eng].append({"op": "drain", "args": (), "kw": {},
                                          "incs": []})
                window = []
            self._last_out[eng] = window + [(outs, ins)]
            if len(self._last_out[eng]) > 48:
                del self._last_out[eng][:-48]
        if inc:
            for s, k in inc:
                s.n += k
        self.streams[eng].append({"op": op, "args": args, "kw": kw,
                                  "incs": list(inc or [])})

    def wait(self, eng, sem, val=None):
        v = sem.n if val is None else val
        if v <= 0:
            return
        self.streams[eng].append({"wait": (sem, v)})

    def dma(self, eng, out, in_, **kw):
        sems = self.dsem[eng]
        sm = sems[self.drot[eng] % len(sems)]
        self.drot[eng] += 1
        self.emit(eng, "dma_start", out=out, in_=in_, inc=[(sm, 16)], **kw)
        return (sm, sm.n)

    def waitt(self, eng, tok):
        if tok is None:
            return
        if isinstance(tok, list):
            for t in tok:
                self.waitt(eng, t)
            return
        sem, v = tok
        self.wait(eng, sem, v)

    def wait_dmas(self, eng, queue):
        for sm in self.dsem[queue]:
            self.wait(eng, sm, sm.n)

    def mark(self, eng):
        for entry in reversed(self.streams[eng]):
            if "op" not in entry:
                continue
            if entry["incs"]:
                s, _ = entry["incs"][-1]
                return (s, s.n)
            p = self.psem[eng]
            p.n += 1
            entry["incs"].append((p, 1))
            return (p, p.n)
        return (self.psem[eng], 0)

    def barrier(self):
        tokens = []
        for e in ENGINES:
            for sm in self.dsem.get(e, []):
                if sm.n > 0:
                    tokens.append((e, sm, sm.n))
            tokens.append((e, *self.mark(e)))
        for e in ENGINES:
            for (src, s, v) in tokens:
                if src != e:
                    self.wait(e, s, v)

    def run(self, eng, e):
        for entry in self.streams[eng]:
            if "wait" in entry:
                sem, v = entry["wait"]
                e.wait_ge(sem.h, v)
            else:
                inst = getattr(e, entry["op"])(*entry["args"], **entry["kw"])
                for s, k in entry["incs"]:
                    inst = inst.then_inc(s.h, k)


def build_program(debug=False, strict=False):
    nc = bass.Bass("TRN2", target_bir_lowering=False,
                   detect_race_conditions=strict)
    es = ExitStack()

    ap = {}

    def inp(name, shape, dt):
        ap[name] = nc.dram_tensor(name, list(shape), dt, kind="ExternalInput").ap()

    inp("xin", (D, TF), F32)
    inp("wkT", (KD, 128, KD * 128), F32)
    inp("wvT", (KD, 128, KD * 128), F32)
    inp("wrT", (KD, 128, KD * 128), F32)
    inp("wgT", (MI, 128, KD * 128), F32)
    inp("wuT", (MI, 128, KD * 128), F32)
    inp("wdT", (KD, 128, MI * 128), F32)
    inp("wg_nat", (I, D), F32)
    inp("wu_nat", (I, D), F32)
    inp("wd_nat", (D, I), F32)
    inp("tdt", (128, KD), F32)
    inp("iota", (1, TM), F32)
    inp("identity", (128, 128), F32)
    inp("id224", (128, 128), F32)
    inp("onesbf", (128, 1), BF16)
    inp("onesf", (128, 128), F32)
    inp("msend", (128, 1), F32)
    inp("mrecv", (128, 1), F32)
    yout = nc.dram_tensor("yout", [D, TM], F32, kind="ExternalOutput").ap()

    qg_c = nc.dram_tensor("qg_cache", [MI, 128, KD * 128], FP8, kind="Internal").ap()
    qu_c = nc.dram_tensor("qu_cache", [MI, 128, KD * 128], FP8, kind="Internal").ap()
    qd_c = nc.dram_tensor("qd_cache", [KD, 128, MI * 128], FP8, kind="Internal").ap()
    wkb_c = nc.dram_tensor("wkb_c", [KD, 128, KD * 128], BF16, kind="Internal").ap()
    wvb_c = nc.dram_tensor("wvb_c", [KD, 128, KD * 128], BF16, kind="Internal").ap()
    wrb_c = nc.dram_tensor("wrb_c", [KD, 128, KD * 128], BF16, kind="Internal").ap()
    cc_in = [nc.dram_tensor(f"cc_in{i}", [128, 4], F32, kind="Internal").ap() for i in range(8)]
    cc_out = [nc.dram_tensor(f"cc_out{i}", [128, 4], F32, kind="Internal").ap() for i in range(8)]
    rowbuf = [nc.dram_tensor(f"rowb{i}", [1, TF], F32, kind="Internal").ap() for i in range(4)]
    RG = [[0, 1], [2, 3], [4, 5], [6, 7]]

    dbg = {}
    if debug:
        for name, shape, dt in (
            ("dbg_qx", (128, KD * TCOL), FP8),
            ("dbg_t", (128, MI * TCOL), BF16),
            ("dbg_x1", (D, TF), F32),
            ("dbg_xa", (D, TF), BF16),
            ("dbg_nd", (128, 2 * TM), F32),
            ("dbg_x2", (D, TF), F32),
        ):
            dbg[name] = nc.dram_tensor(name, list(shape), dt, kind="ExternalOutput").ap()

    def sb(name, shape, dt):
        return es.enter_context(nc.sbuf_tensor(name, list(shape), dt))

    xt = sb("xt", (128, KD * TF), F32)
    sbf = sb("sbf", (128, 16 * TF + 2 * TCOL), BF16)
    sf8 = sb("sf8", (128, KD * TCOL + MI * TCOL), FP8)
    scr = sb("scr", (128, 9330), F32)
    ndr = sb("ndr", (128, 8 * TM), BF16)
    wring = sb("wring", (128, 44 * 128), F32)
    qwr = sb("qwr", (128, 64 * 128), FP8)
    bfr = wring[:].bitcast(BF16)
    dqw = wring[:].bitcast(FP8)
    natw = ndr[:].bitcast(F32)
    upstg = sf8[:].bitcast(F32)
    UPS0 = (KD * TCOL) // 4 + 2
    smalls = sb("smalls", (128, 256), F32)
    idt = sb("idt", (128, 128), F32)
    id224 = sb("id224t", (128, 128), F32)
    onesbf = sb("onesbft", (128, 1), BF16)
    onesf = sb("onesft", (128, 128), F32)

    TMP, CMO = 0, 1025
    BC0, BC1 = 2050, 3075
    GO, UO, SGO, T1O = 4100, 4442, 4784, 5126
    RHO, TAU = 5468, 5810
    ROWS, ROWS2, ROWS3 = 6152, 7208, 8264
    EKO, AO = TMP, CMO
    WREP = BC0
    IOTA_B = BC1
    DECM = GO

    S_WSG, S_WSU, S_WSD = 0, 44, 88
    S_W, S_LOGW, S_TD, S_E1 = 104, 120, 136, 224
    S_MSK_S, S_MSK_R = 152, 153
    S_SST, S_SGT = 160, 176

    _ROWMAP = {0: (6152, 0), 1: (6152, 32), 4: (6152, 64),
               3: (7208, 0), 2: (7208, 32),
               5: (8264, 0), 6: (8264, 32)}

    def row(i):
        base, p = _ROWMAP[i]
        return scr[p:p + 1, base:base + 1025]
    R_MEAN, R_RSTD, R_ZMAX, R_C1, R_C0, R_AUX, R_AUX2 = range(7)

    CVO = 8264 + 1030

    def psum(name):
        return es.enter_context(nc.psum_tensor(name, [128, 512], F32))

    ps_a, ps_b = psum("ps_a"), psum("ps_b")
    ps_c, ps_d = psum("ps_c"), psum("ps_d")
    ps_e, ps_f = psum("ps_e"), psum("ps_f")
    ps_aux = psum("ps_aux")

    B = Builder(nc, es, strict=strict)
    pT, pV, pS, pG = B.psem["tensor"], B.psem["vector"], B.psem["scalar"], B.psem["gpsimd"]

    def xsl(k, c0, w):
        return xt[:, k * TF + c0: k * TF + c0 + w]

    def xasl(k, c0, w):
        return sbf[:, k * TF + c0: k * TF + c0 + w]

    def tbsl(m, w):
        return sbf[:, m * TCOL: m * TCOL + w]

    def qxsl(k, w):
        return sf8[:, k * TCOL: k * TCOL + w]

    def qtsl(m, w):
        return sf8[:, KD * TCOL + m * TCOL: KD * TCOL + m * TCOL + w]

    def scrs(off, w):
        return scr[:, off:off + w]

    pp_jobs = []
    q3 = ("sync", "scalar", "gpsimd")
    for m in range(MI):
        pp_jobs.append((ap["wg_nat"][m * 128:(m + 1) * 128, :], S_WSG + m, 1.0, True))
        pp_jobs.append((ap["wu_nat"][m * 128:(m + 1) * 128, :], S_WSU + m, 1.0, True))
    for j in range(KD):
        for p in range(3):
            pp_jobs.append((ap["wd_nat"][j * 128:(j + 1) * 128, p * 2048:min((p + 1) * 2048, I)],
                            S_WSD + j, 2.0, p == 2))
    pp = {"emitted": 0, "issued": 0, "toks": {}, "red": {}, "acc_first": True,
          "done": {}}

    def _pp_issue(i):
        src = pp_jobs[i][0]
        slot = (i % 2) * 2048
        B.wait(q3[i % 3], pV, pp["red"].get(i - 2, 0))
        pp["toks"][i] = B.dma(q3[i % 3], natw[:, slot:slot + src.shape[1]], src)
        pp["issued"] = i + 1

    def prepass_until(njobs):
        while pp["emitted"] < min(njobs, len(pp_jobs)):
            i = pp["emitted"]
            while pp["issued"] < min(i + 2, len(pp_jobs)):
                _pp_issue(pp["issued"])
            src, ws_col, sc, fin = pp_jobs[i]
            slot = (i % 2) * 2048
            B.waitt("vector", pp["toks"].pop(i))
            B.emit("vector", "tensor_reduce", scr[:, CVO:CVO + 1], natw[:, slot:slot + src.shape[1]],
                   AxX, Alu.max, apply_absolute_value=True)
            if ws_col < S_WSD:
                B.emit("vector", "tensor_scalar_max", scr[:, CVO:CVO + 1], scr[:, CVO:CVO + 1], 1e-30)
                B.emit("vector", "tensor_scalar_mul", smalls[:, ws_col:ws_col + 1], scr[:, CVO:CVO + 1],
                       1.0 / (224.0 * sc), inc=[(pV, 1)])
            else:
                if pp["acc_first"]:
                    B.emit("vector", "tensor_copy", scr[:, CVO + 4:CVO + 5], scr[:, CVO:CVO + 1])
                else:
                    B.emit("vector", "tensor_max", scr[:, CVO + 4:CVO + 5], scr[:, CVO + 4:CVO + 5],
                           scr[:, CVO:CVO + 1])
                pp["acc_first"] = False
                if fin:
                    B.emit("vector", "tensor_scalar_max", scr[:, CVO + 4:CVO + 5], scr[:, CVO + 4:CVO + 5], 1e-30)
                    B.emit("vector", "tensor_scalar_mul", smalls[:, ws_col:ws_col + 1], scr[:, CVO + 4:CVO + 5],
                           1.0 / (224.0 * sc), inc=[(pV, 1)])
                    pp["acc_first"] = True
                else:
                    B.emit("vector", "engine_nop", inc=[(pV, 1)])
            pp["red"][i] = pV.n
            pp["done"][i] = pV.n
            pp["emitted"] = i + 1
            if pp["issued"] < min(i + 3, len(pp_jobs)):
                _pp_issue(pp["issued"])

    _XQ = ("sync", "scalar", "gpsimd")
    xtoks = []
    for k in range(KD):
        xtoks.append(B.dma(_XQ[k % 3], xsl(k, 0, TF), ap["xin"][k * 128:(k + 1) * 128, :]))
    B.dma("sync", smalls[:, S_TD:S_TD + KD], ap["tdt"][:])
    B.dma("sync", smalls[:, S_MSK_S:S_MSK_S + 1], ap["msend"][:])
    B.dma("sync", smalls[:, S_MSK_R:S_MSK_R + 1], ap["mrecv"][:])
    B.dma("sync", idt[:], ap["identity"][:])
    B.dma("sync", id224[:], ap["id224"][:])
    B.dma("sync", onesbf[:], ap["onesbf"][:])
    B.dma("sync", onesf[:], ap["onesf"][:])
    for f in range(KD):
        B.dma("gpsimd", wkb_c[f], ap["wkT"][f])
        B.dma("gpsimd", wvb_c[f], ap["wvT"][f])
        B.dma("gpsimd", wrb_c[f], ap["wrT"][f])
    B.wait_dmas("scalar", "sync")
    B.emit("scalar", "activation", smalls[:, S_E1:S_E1 + KD], smalls[:, S_TD:S_TD + KD], Act.Exp, inc=[(pS, 1)])
    B.wait("vector", pS)
    B.emit("vector", "tensor_scalar_mul", smalls[:, S_LOGW:S_LOGW + KD], smalls[:, S_E1:S_E1 + KD], -1.0, inc=[(pV, 1)])
    B.wait("scalar", pV)
    B.emit("scalar", "activation", smalls[:, S_W:S_W + KD], smalls[:, S_LOGW:S_LOGW + KD], Act.Exp, inc=[(pS, 1)])
    B.barrier()

    def ln_stats(cols):
        c0, w = cols
        sq = [sbf[:, 16 * TF:16 * TF + TCOL], sbf[:, 16 * TF + TCOL:16 * TF + 2 * TCOL]]
        for k in range(KD):
            B.emit("vector", "tensor_copy", xasl(k, c0, w), xsl(k, c0, w), inc=[(pV, 1)])
        vcopy = pV.n
        B.wait("scalar", pV, vcopy)
        ccs = [(c0 + i, min(342, w - i)) for i in range(0, w, 342)]
        evac_done = 0
        for sc0, sw in ccs:
            mm_prev = []
            for k in range(KD):
                s = sq[k % 2]
                if mm_prev:
                    B.wait("scalar", pT, mm_prev[-1] if k >= 2 else 0)
                B.emit("scalar", "activation", s[:, :sw], xasl(k, sc0, sw), Act.Square, inc=[(pS, 1)])
                B.wait("tensor", pS)
                if k == 0:
                    B.wait("tensor", pV, evac_done)
                B.emit("tensor", "matmul", ps_e[0:1, 0:sw], onesbf[:], xasl(k, sc0, sw),
                       start=(k == 0), stop=(k == KD - 1))
                B.emit("tensor", "matmul", ps_f[0:1, 0:sw], onesbf[:], s[:, :sw],
                       start=(k == 0), stop=(k == KD - 1), inc=[(pT, 1)])
                mm_prev.append(pT.n)
            B.wait("vector", pT)
            r0 = sc0 - c0
            mean = row(R_MEAN)[:, r0:r0 + sw]
            rstd = row(R_RSTD)[:, r0:r0 + sw]
            aux = row(R_AUX)[:, r0:r0 + sw]
            aux2 = row(R_AUX2)[:, r0:r0 + sw]
            B.emit("vector", "tensor_scalar_mul", mean, ps_e[0:1, 0:sw], 1.0 / D)
            B.emit("vector", "tensor_mul", aux2, mean, mean)
            B.emit("vector", "scalar_tensor_tensor", aux, ps_f[0:1, 0:sw], 1.0 / D, aux2, Alu.mult, Alu.subtract)
            B.emit("vector", "tensor_scalar_add", aux, aux, 1e-5)
            B.emit("vector", "reciprocal", aux2, aux, inc=[(pV, 1)])
            evac_done = pV.n
            B.wait("scalar", pV, evac_done)
            B.emit("scalar", "activation", rstd, aux2, Act.Sqrt, inc=[(pS, 1)])
        return pS.n

    _rb = [0]

    def bcast(row_ap, dst_off, w):
        i = _rb[0] % 4
        _rb[0] += 1
        d1 = B.dma("sync", rowbuf[i][:, 0:w], row_ap)
        B.waitt("sync", d1)
        B.wait("sync", pV)
        return B.dma("sync", scrs(dst_off, w), rowbuf[i][:, 0:w].broadcast_to([128, w]))

    def colmax_finish(cm_off, cm_ready_pv, w, out_row_i, op=Alu.max, negate=False):
        B.wait("tensor", pV, cm_ready_pv)
        out_r = row(out_row_i)
        red_done = 0
        dma_done = None
        for j in range((w + 127) // 128):
            c = min(128, w - j * 128)
            B.wait("tensor", pV, red_done)
            B.emit("tensor", "transpose", ps_aux[0:c, 0:128], scrs(cm_off + j * 128, c), idt[:], inc=[(pT, 1)])
            B.wait("vector", pT)
            B.waitt("vector", dma_done)
            B.emit("vector", "tensor_reduce", scr[0:c, CVO:CVO + 1], ps_aux[0:c, 0:128], AxX, op,
                   negate=negate, inc=[(pV, 1)])
            red_done = pV.n
            B.wait("sync", pV, red_done)
            dma_done = B.dma("sync", out_r[:, j * 128:j * 128 + c], scr[0:c, CVO:CVO + 1])
        B.waitt("vector", dma_done)
        return pV.n

    def prep_weight_chunk(srcT_chunk, nkt, stage_ap, stage_tok, qoff, ws_col, sc, cache_chunk):
        B.emit("vector", "reciprocal", scr[:, CVO + 6:CVO + 7], smalls[:, ws_col:ws_col + 1])
        B.emit("vector", "tensor_scalar", scrs(T1O, 128), idt[:], scr[:, CVO + 6:CVO + 7],
               1.0 / sc, Alu.mult, Alu.mult, inc=[(pV, 1)])
        B.wait("tensor", pV)
        B.wait("tensor", pS)
        B.emit("tensor", "matmul", ps_aux[:, 256:384], onesf[:], scrs(T1O, 128), start=True, stop=True, inc=[(pT, 1)])
        B.wait("vector", pT)
        B.emit("vector", "tensor_copy", scrs(T1O, 128), ps_aux[:, 256:384], inc=[(pV, 1)])
        B.waitt("vector", stage_tok)
        B.wait_dmas("vector", "gpsimd")
        B.wait("vector", pT)
        rb = scrs(T1O, 128).rearrange("p (o n) -> p o n", o=1).broadcast_to([128, nkt, 128])
        sv = stage_ap[:, 0:nkt * 128].rearrange("p (k n) -> p k n", k=nkt)
        ov = qwr[:, qoff:qoff + nkt * 128].rearrange("p (k n) -> p k n", k=nkt)
        B.emit("vector", "tensor_mul", ov, sv, rb, inc=[(pV, 1)])
        q_done = pV.n
        B.wait("gpsimd", pV, q_done)
        st = B.dma("gpsimd", cache_chunk, qwr[:, qoff:qoff + nkt * 128])
        return q_done, st

    def emit_ffn(call):
        thirds = THIRDS_FULL if call == 1 else THIRDS_MAIN
        c0, w = (0, TF) if call == 1 else (1, TM)
        stats_ps = ln_stats((c0, w))
        for k in range(KD):
            xs_ = xsl(k, c0, w)
            B.emit("vector", "tensor_max", scrs(CMO, w), xs_ if k == 0 else scrs(CMO, w), xs_)
            B.emit("vector", "tensor_tensor", scrs(TMP, w), xs_ if k == 0 else scrs(TMP, w), xs_, Alu.min, inc=[(pV, 1)])
        tree_pv = pV.n
        colmax_finish(CMO, tree_pv, w, R_C1)
        mdone = colmax_finish(TMP, pV.n, w, R_AUX, op=Alu.min, negate=True)
        B.wait("vector", pS, stats_ps)
        B.wait("vector", pV, mdone)
        B.emit("vector", "tensor_sub", row(R_ZMAX)[:, 0:w], row(R_C1)[:, 0:w], row(R_MEAN)[:, 0:w])
        B.emit("vector", "tensor_add", row(R_AUX2)[:, 0:w], row(R_AUX)[:, 0:w], row(R_MEAN)[:, 0:w])
        B.emit("vector", "tensor_max", row(R_ZMAX)[:, 0:w], row(R_ZMAX)[:, 0:w], row(R_AUX2)[:, 0:w])
        B.emit("vector", "tensor_scalar_max", row(R_ZMAX)[:, 0:w], row(R_ZMAX)[:, 0:w], 1e-30, inc=[(pV, 1)])
        B.emit("vector", "reciprocal", row(R_AUX)[:, 0:w], row(R_ZMAX)[:, 0:w])
        B.emit("vector", "tensor_scalar_mul", row(R_C1)[:, 0:w], row(R_RSTD)[:, 0:w], 224.0)
        B.emit("vector", "tensor_mul", row(R_C1)[:, 0:w], row(R_C1)[:, 0:w], row(R_AUX)[:, 0:w])
        B.emit("vector", "scalar_tensor_tensor", row(R_C0)[:, 0:w], row(R_MEAN)[:, 0:w], -1.0, row(R_C1)[:, 0:w], Alu.mult, Alu.mult)
        B.emit("vector", "tensor_scalar_mul", row(R_ZMAX)[:, 0:w], row(R_ZMAX)[:, 0:w], 1.0 / 224.0, inc=[(pV, 1)])
        B.wait("sync", pV)
        b1 = bcast(row(R_C1)[:, 0:w], BC1, w)
        b2 = bcast(row(R_C0)[:, 0:w], BC0, w)

        for ti, (t0, tw) in enumerate(thirds):
            toff = t0 - c0
            prep = (call == 1 and ti == 0)
            B.waitt("vector", [b1, b2])
            for k in range(KD):
                B.emit("vector", "tensor_mul", scrs(TMP, tw), xsl(k, t0, tw), scr[:, BC1 + toff:BC1 + toff + tw])
                B.emit("vector", "tensor_add", qxsl(k, tw), scrs(TMP, tw), scr[:, BC0 + toff:BC0 + toff + tw], inc=[(pV, 1)])
            qx_done = pV.n
            rho_dma = bcast(row(R_ZMAX)[:, toff:toff + tw], RHO, tw)
            if debug and call == 1 and ti == 0:
                B.wait("gpsimd", pV, qx_done)
                for k in range(KD):
                    B.dma("gpsimd", dbg["dbg_qx"][:, k * TCOL:k * TCOL + tw], qxsl(k, tw))

            evac_done = {}
            mm_done = {}
            load_done = {}
            stage_toks = {}
            sig_done = {}
            abs_done = {}

            def stage_gu(m):
                gsl = wring[:, (m % 2) * 2048:(m % 2) * 2048 + 2048]
                usl = upstg[:, UPS0:UPS0 + 2048]
                t1_ = B.dma("sync", gsl, ap["wgT"][m])
                if m > 0:
                    B.wait("scalar", pV)
                t2_ = B.dma("scalar", usl, ap["wuT"][m])
                stage_toks[m] = (gsl, usl, [t1_, t2_])

            def gu_load(m):
                qb = (m % 2) * 32 * 128
                B.wait("scalar", pT, mm_done.get(m - 2, 0))
                t1_ = B.dma("scalar", qwr[:, qb:qb + 16 * 128], qg_c[m])
                t2_ = B.dma("scalar", qwr[:, qb + 16 * 128:qb + 32 * 128], qu_c[m])
                load_done[m] = [t1_, t2_]

            if not prep:
                B.wait_dmas("scalar", "gpsimd")
                B.wait("scalar", pT)
                gu_load(0)
                gu_load(1)
            else:
                stage_gu(0)
            cm_ready = 0
            for m in range(MI):
                pg = ps_a if m % 2 == 0 else ps_b
                pu = ps_c if m % 2 == 0 else ps_d
                qb = (m % 2) * 32 * 128
                if prep:
                    prepass_until(2 * (min(m + 4, MI)))
                    gsl, usl, stoks = stage_toks.pop(m)
                    _q, _st0 = prep_weight_chunk(ap["wgT"][m], KD, gsl, [stoks[0]], qb, S_WSG + m, 1.0, qg_c[m])
                    wready, _st = prep_weight_chunk(ap["wuT"][m], KD, usl, [stoks[1]], qb + 16 * 128,
                                                    S_WSU + m, 1.0, qu_c[m])
                    if m + 1 < MI and (m + 1) not in stage_toks:
                        B.wait("sync", pV, wready)
                        stage_gu(m + 1)
                    elif m + 2 < MI:
                        B.wait("sync", pV, wready)
                        stage_gu(m + 2)
                    B.wait("tensor", pV, max(wready, qx_done))
                else:
                    B.waitt("tensor", load_done[m])
                    B.wait("tensor", pV, qx_done)
                B.wait("tensor", pV, evac_done.get(m - 2, 0))
                for kk in range(0, KD, 2):
                    lhsT = qwr[:, qb + kk * 128:qb + (kk + 2) * 128].rearrange("p (ko d) -> p ko d", ko=2)
                    rhs = sf8[:, kk * TCOL:(kk + 2) * TCOL].rearrange("p (ko n) -> p ko n", ko=2)[:, :, 0:tw]
                    B.emit("tensor", "matmul", pg[:, 0:tw], lhsT, rhs,
                           start=(kk == 0), stop=(kk == KD - 2), perf_mode=PM_DR,
                           inc=[(pT, 1)] if kk == KD - 2 else None)
                for kk in range(0, KD, 2):
                    lhsT = qwr[:, qb + (16 + kk) * 128:qb + (18 + kk) * 128].rearrange("p (ko d) -> p ko d", ko=2)
                    rhs = sf8[:, kk * TCOL:(kk + 2) * TCOL].rearrange("p (ko n) -> p ko n", ko=2)[:, :, 0:tw]
                    B.emit("tensor", "matmul", pu[:, 0:tw], lhsT, rhs,
                           start=(kk == 0), stop=(kk == KD - 2), perf_mode=PM_DR,
                           inc=[(pT, 1)] if kk == KD - 2 else None)
                mm_done[m] = pT.n
                if (not prep) and m + 2 < MI:
                    gu_load(m + 2)
                if m >= 1:
                    j = m - 1
                    puj = ps_c if j % 2 == 0 else ps_d
                    B.wait("vector", pT, mm_done[j])
                    B.wait("vector", pS, sig_done[j])
                    B.emit("vector", "tensor_mul", scrs(T1O, tw),
                           scrs(GO if j % 2 == 0 else UO, tw), scrs(SGO if j % 2 == 0 else TMP, tw))
                    B.emit("vector", "scalar_tensor_tensor", tbsl(j, tw), puj[:, 0:tw],
                           smalls[:, S_WSU + j:S_WSU + j + 1], scrs(T1O, tw), Alu.mult, Alu.mult, inc=[(pV, 1)])
                    evac_done[j] = pV.n
                    B.wait("scalar", pV, evac_done[j])
                    B.emit("scalar", "activation", scrs(GO if j % 2 == 0 else UO, tw),
                           tbsl(j, tw), Act.Abs, inc=[(pS, 1)])
                    abs_done[j] = pS.n
                if m >= 2:
                    j = m - 2
                    B.wait("vector", pS, abs_done[j])
                    src_ = scrs(GO if j % 2 == 0 else UO, tw)
                    if j == 0:
                        B.emit("vector", "tensor_copy", scrs(CMO, tw), src_, inc=[(pV, 1)])
                    else:
                        B.emit("vector", "tensor_max", scrs(CMO, tw), scrs(CMO, tw), src_, inc=[(pV, 1)])
                B.wait("vector", pT, mm_done[m])
                B.waitt("vector", rho_dma)
                B.emit("vector", "scalar_tensor_tensor", scrs(GO if m % 2 == 0 else UO, tw), pg[:, 0:tw],
                       smalls[:, S_WSG + m:S_WSG + m + 1], scrs(RHO, tw), Alu.mult, Alu.mult, inc=[(pV, 1)])
                B.wait("scalar", pV, pV.n)
                B.emit("scalar", "activation", scrs(SGO if m % 2 == 0 else TMP, tw),
                       scrs(GO if m % 2 == 0 else UO, tw), Act.Sigmoid, inc=[(pS, 1)])
                sig_done[m] = pS.n
            for j in (MI - 1,):
                puj = ps_c if j % 2 == 0 else ps_d
                B.wait("vector", pT, mm_done[j])
                B.wait("vector", pS, sig_done[j])
                B.emit("vector", "tensor_mul", scrs(T1O, tw),
                       scrs(GO if j % 2 == 0 else UO, tw), scrs(SGO if j % 2 == 0 else TMP, tw))
                B.emit("vector", "scalar_tensor_tensor", tbsl(j, tw), puj[:, 0:tw],
                       smalls[:, S_WSU + j:S_WSU + j + 1], scrs(T1O, tw), Alu.mult, Alu.mult, inc=[(pV, 1)])
                evac_done[j] = pV.n
                B.wait("scalar", pV, evac_done[j])
                B.emit("scalar", "activation", scrs(GO if j % 2 == 0 else UO, tw),
                       tbsl(j, tw), Act.Abs, inc=[(pS, 1)])
                abs_done[j] = pS.n
            for j in (MI - 2, MI - 1):
                B.wait("vector", pS, abs_done[j])
                B.emit("vector", "tensor_max", scrs(CMO, tw), scrs(CMO, tw),
                       scrs(GO if j % 2 == 0 else UO, tw), inc=[(pV, 1)])
            cm_ready = pV.n
            tdone = colmax_finish(CMO, cm_ready, tw, R_AUX2)
            B.wait("vector", pV, tdone)
            B.emit("vector", "tensor_scalar_max", row(R_AUX2)[:, 0:tw], row(R_AUX2)[:, 0:tw], 1e-30)
            B.emit("vector", "reciprocal", row(R_AUX)[:, 0:tw], row(R_AUX2)[:, 0:tw])
            B.emit("vector", "tensor_scalar_mul", row(R_C1)[:, 0:tw], row(R_AUX)[:, 0:tw], 224.0)
            B.emit("vector", "tensor_mul", row(R_AUX2)[:, 0:tw], row(R_AUX2)[:, 0:tw],
                   row(R_ZMAX)[:, toff:toff + tw])
            B.emit("vector", "tensor_scalar_mul", row(R_AUX2)[:, 0:tw], row(R_AUX2)[:, 0:tw], 1.0 / 224.0, inc=[(pV, 1)])
            B.wait("sync", pV)
            B.waitt("sync", rho_dma)
            qrho = bcast(row(R_C1)[:, 0:tw], RHO, tw)
            qdma = bcast(row(R_AUX2)[:, 0:tw], TAU, tw)
            B.waitt("vector", [qrho, qdma])
            for m in range(MI):
                B.emit("vector", "tensor_mul", qtsl(m, tw), tbsl(m, tw), scrs(RHO, tw), inc=[(pV, 1)])
            qt_done = pV.n
            if debug and call == 1 and ti == 0:
                B.wait("gpsimd", pV, qt_done)
                for m in range(MI):
                    B.dma("gpsimd", dbg["dbg_t"][:, m * TCOL:m * TCOL + tw], tbsl(m, tw))

            if prep:
                prepass_until(len(pp_jobs))
            ev_y = {}
            mmd_done = {}
            dload = {}

            def d_load(j):
                qb = (j % 2) * 44 * 128
                B.wait("scalar", pT, mmd_done.get(j - 2, 0))
                dload[j] = B.dma("scalar", dqw[:, qb:qb + 44 * 128], qd_c[j])

            if not prep:
                B.wait("scalar", pV, qt_done)
                B.wait_dmas("scalar", "gpsimd")
                d_load(0)
                d_load(1)
            for j in range(KD):
                py = ps_e if j % 2 == 0 else ps_f
                if prep:
                    B.wait("sync", pV)
                    B.wait("scalar", pV)
                    dt1 = B.dma("sync", wring[:, 0:22 * 128], ap["wdT"][j][:, 0:22 * 128])
                    dt2 = B.dma("scalar", wring[:, 22 * 128:44 * 128], ap["wdT"][j][:, 22 * 128:44 * 128])
                    wready, _st = prep_weight_chunk(ap["wdT"][j], MI, wring[:, 0:44 * 128],
                                                    [dt1, dt2], 0, S_WSD + j, 2.0, qd_c[j])
                    B.wait("tensor", pV, max(wready, qt_done))
                    src = qwr
                    qb = 0
                else:
                    src = dqw
                    qb = (j % 2) * 44 * 128
                    B.waitt("tensor", dload[j])
                    B.wait("tensor", pV, qt_done)
                B.wait("tensor", pV, ev_y.get(j - 2, 0))
                for mk in range(0, MI, 2):
                    lhsT = src[:, qb + mk * 128:qb + (mk + 2) * 128].rearrange("p (ko d) -> p ko d", ko=2)
                    rhs = sf8[:, KD * TCOL + mk * TCOL:KD * TCOL + (mk + 2) * TCOL].rearrange(
                        "p (ko n) -> p ko n", ko=2)[:, :, 0:tw]
                    B.emit("tensor", "matmul", py[:, 0:tw], lhsT, rhs,
                           start=(mk == 0), stop=(mk == MI - 2), perf_mode=PM_DR,
                           inc=[(pT, 1)] if mk == MI - 2 else None)
                mmd_done[j] = pT.n
                if (not prep) and j + 2 < KD:
                    d_load(j + 2)
                B.wait("vector", pT, mmd_done[j])
                B.emit("vector", "scalar_tensor_tensor", scrs(T1O, tw), py[:, 0:tw],
                       smalls[:, S_WSD + j:S_WSD + j + 1], scrs(TAU, tw), Alu.mult, Alu.mult)
                B.emit("vector", "tensor_add", xsl(j, t0, tw), xsl(j, t0, tw), scrs(T1O, tw), inc=[(pV, 1)])
                ev_y[j] = pV.n
        B.barrier()

    def emit_attn():
        stats_ps = ln_stats((0, TF))
        B.wait("sync", pS, stats_ps)
        b1 = bcast(row(R_MEAN)[:, 0:TF], BC1, TF)
        b2 = bcast(row(R_RSTD)[:, 0:TF], BC0, TF)
        B.waitt("vector", [b1, b2])
        for k in range(KD):
            B.emit("vector", "tensor_sub", scrs(TMP, TF), xsl(k, 0, TF), scrs(BC1, TF))
            B.emit("vector", "tensor_mul", xasl(k, 0, TF), scrs(TMP, TF), scrs(BC0, TF), inc=[(pV, 1)])
        xa_done = pV.n
        if debug:
            B.wait("gpsimd", pV, xa_done)
            for k in range(KD):
                B.dma("gpsimd", dbg["dbg_xa"][k * 128:(k + 1) * 128, :], xasl(k, 0, TF))
        B.wait("sync", pV)
        bio = B.dma("sync", scrs(IOTA_B, TM), ap["iota"].broadcast_to([128, TM]))

        KS = [0, 2048]
        VS = [4096, 6144]
        RS = 8192
        kv_dma = {}
        k_used = {}
        v_used = {}
        r_used = [0]

        def kv_load(f):
            B.wait("sync", pT, k_used.get(f - 2, 0))
            t1_ = B.dma("sync", bfr[:, KS[f % 2]:KS[f % 2] + 2048], wkb_c[f])
            B.wait("sync", pT, v_used.get(f - 2, 0))
            t2_ = B.dma("sync", bfr[:, VS[f % 2]:VS[f % 2] + 2048], wvb_c[f])
            kv_dma[f] = [t1_, t2_]

        ek_done = {}
        amul_done = {}
        scans_done = {}
        scan_slot_free = {}
        cc_done = {}
        stash_done = {}
        ekao_free = [0]

        def attn_chunk(f):
            B.waitt("tensor", kv_dma[f])
            B.wait("tensor", pV, xa_done)
            B.wait("tensor", pS, ek_done.get(f - 1, 0))
            B.wait("tensor", pV, amul_done.get(f - 1, 0))
            base = KS[f % 2]
            for cc in range(2):
                for kk in range(KD):
                    B.emit("tensor", "matmul", (ps_a if cc == 0 else ps_b)[:, 0:512],
                           bfr[:, base + kk * 128:base + (kk + 1) * 128],
                           xasl(kk, cc * 512, 512),
                           start=(kk == 0), stop=(kk == KD - 1),
                           inc=[(pT, 1)] if kk == KD - 1 else None)
            k_done = pT.n
            k_used[f] = k_done
            vbase = VS[f % 2]
            for cc in range(2):
                for kk in range(KD):
                    B.emit("tensor", "matmul", (ps_c if cc == 0 else ps_d)[:, 0:512],
                           bfr[:, vbase + kk * 128:vbase + (kk + 1) * 128],
                           xasl(kk, cc * 512, 512),
                           start=(kk == 0), stop=(kk == KD - 1),
                           inc=[(pT, 1)] if kk == KD - 1 else None)
            v_done = pT.n
            v_used[f] = v_done
            if f + 2 < KD:
                kv_load(f + 2)
            B.wait("scalar", pT, k_done)
            B.wait("scalar", pV, max(scans_done.get(f - 1, 0), ekao_free[0]))
            B.emit("scalar", "activation", scrs(EKO, 512), ps_a[:, 0:512], Act.Exp)
            B.emit("scalar", "activation", scrs(EKO + 512, 512), ps_b[:, 0:512], Act.Exp, inc=[(pS, 1)])
            ek_done[f] = pS.n
            B.wait("vector", pS, ek_done[f])
            B.wait("vector", pT, v_done)
            B.wait("vector", pV, scan_slot_free.get(f - 4, 0))
            B.emit("vector", "tensor_mul", scrs(AO, 512), scrs(EKO, 512), ps_c[:, 0:512])
            B.emit("vector", "tensor_mul", scrs(AO + 512, 512), scrs(EKO + 512, 512), ps_d[:, 0:512], inc=[(pV, 1)])
            amul_done[f] = pV.n
            B.emit("vector", "memset", scrs(WREP, TM), 1.0)
            B.emit("vector", "tensor_scalar_mul", scrs(WREP, TM), scrs(WREP, TM), smalls[:, S_W + f:S_W + f + 1])
            slot = (f % 4) * 2 * TM
            B.emit("vector", "tensor_tensor_scan", ndr[:, slot:slot + TM], scrs(WREP, TM), scrs(AO, TM), 0.0, Alu.mult, Alu.add)
            B.emit("vector", "tensor_tensor_scan", ndr[:, slot + TM:slot + 2 * TM], scrs(WREP, TM), scrs(EKO, TM), 0.0, Alu.mult, Alu.add, inc=[(pV, 1)])
            scans_done[f] = pV.n
            half = f // 2
            sst = S_SST + 8 * (half % 2)
            B.waitt("vector", cc_done.get(half - 2))
            B.emit("vector", "tensor_scalar_mul", smalls[:, sst + 2 * (f % 2):sst + 2 * (f % 2) + 1],
                   ndr[:, slot + TM - 1:slot + TM], smalls[:, S_MSK_S:S_MSK_S + 1])
            B.emit("vector", "tensor_scalar_mul", smalls[:, sst + 2 * (f % 2) + 1:sst + 2 * (f % 2) + 2],
                   ndr[:, slot + 2 * TM - 1:slot + 2 * TM], smalls[:, S_MSK_S:S_MSK_S + 1], inc=[(pV, 1)])
            stash_done[half] = pV.n

        def issue_coll(half):
            sst = S_SST + 8 * (half % 2)
            sgt = S_SGT + 8 * (half % 2)
            B.wait("gpsimd", pV, stash_done[half])
            ci_tok = B.dma("gpsimd", cc_in[half][:, 0:4], smalls[:, sst:sst + 4])
            B.waitt("gpsimd", ci_tok)
            B.emit("gpsimd", "collective_compute", "AllReduce", Alu.add,
                   ins=[cc_in[half][:]], outs=[cc_out[half][:]], replica_groups=RG, inc=[(pG, 1)])
            B.wait("gpsimd", pG)
            cc_done[half] = B.dma("gpsimd", smalls[:, sgt:sgt + 4], cc_out[half][:, 0:4])

        rload = {}

        def finalize(f):
            half = f // 2
            slot = (f % 4) * 2 * TM
            sgt = S_SGT + 8 * (half % 2)
            B.wait("scalar", pT, r_used[0])
            rload[f] = B.dma("scalar", bfr[:, RS:RS + 2048], wrb_c[f])
            B.waitt("tensor", rload[f])
            B.wait("tensor", pV, scan_slot_free.get(f - 1, 0))
            for cc in range(2):
                for kk in range(KD):
                    B.emit("tensor", "matmul", (ps_e if cc == 0 else ps_f)[:, 0:512],
                           bfr[:, RS + kk * 128:RS + (kk + 1) * 128],
                           xasl(kk, 1 + cc * 512, 512),
                           start=(kk == 0), stop=(kk == KD - 1),
                           inc=[(pT, 1)] if kk == KD - 1 else None)
            r_done = pT.n
            r_used[0] = r_done
            B.waitt("vector", cc_done[half])
            B.emit("vector", "tensor_scalar_mul", smalls[:, S_SGT + 16:S_SGT + 17],
                   smalls[:, sgt + 2 * (f % 2):sgt + 2 * (f % 2) + 1], smalls[:, S_MSK_R:S_MSK_R + 1])
            B.emit("vector", "tensor_scalar_mul", smalls[:, S_SGT + 17:S_SGT + 18],
                   smalls[:, sgt + 2 * (f % 2) + 1:sgt + 2 * (f % 2) + 2], smalls[:, S_MSK_R:S_MSK_R + 1], inc=[(pV, 1)])
            corr_done = pV.n
            B.waitt("scalar", bio)
            B.wait("scalar", pV, corr_done)
            B.emit("scalar", "activation", scrs(DECM, TM), scrs(IOTA_B, TM), Act.Exp,
                   scale=smalls[:, S_LOGW + f:S_LOGW + f + 1], inc=[(pS, 1)])
            B.wait("vector", pS, pS.n)
            B.emit("vector", "scalar_tensor_tensor", scrs(EKO, TM), scrs(DECM, TM),
                   smalls[:, S_SGT + 16:S_SGT + 17], ndr[:, slot:slot + TM], Alu.mult, Alu.add)
            B.emit("vector", "scalar_tensor_tensor", scrs(AO, TM), scrs(DECM, TM),
                   smalls[:, S_SGT + 17:S_SGT + 18], ndr[:, slot + TM:slot + 2 * TM], Alu.mult, Alu.add)
            B.emit("vector", "tensor_scalar_add", scrs(AO, TM), scrs(AO, TM), 1e-8)
            B.emit("vector", "reciprocal", scrs(WREP, TM), scrs(AO, TM))
            B.emit("vector", "tensor_mul", scrs(EKO, TM), scrs(EKO, TM), scrs(WREP, TM), inc=[(pV, 1)])
            wkv_done = pV.n
            if debug and f == 1:
                B.wait("gpsimd", pV, wkv_done)
                B.dma("gpsimd", dbg["dbg_nd"][:, 0:TM], scrs(EKO, TM))
                B.dma("gpsimd", dbg["dbg_nd"][:, TM:2 * TM], scrs(AO, TM))
                B.wait_dmas("vector", "gpsimd")
            B.wait("vector", pT, r_done)
            B.emit("vector", "tensor_mul", scrs(AO, 512), ps_e[:, 0:512], scrs(EKO, 512))
            B.emit("vector", "tensor_mul", scrs(AO + 512, 512), ps_f[:, 0:512], scrs(EKO + 512, 512))
            B.emit("vector", "tensor_add", xsl(f, 1, TM), xsl(f, 1, TM), scrs(AO, TM), inc=[(pV, 1)])
            scan_slot_free[f] = pV.n
            ekao_free[0] = pV.n

        kv_load(0)
        kv_load(1)
        for half in range(8):
            attn_chunk(2 * half)
            attn_chunk(2 * half + 1)
            issue_coll(half)
            if half > 0:
                finalize(2 * half - 2)
                finalize(2 * half - 1)
        finalize(14)
        finalize(15)
        B.barrier()

    emit_ffn(1)
    if debug:
        for k in range(KD):
            B.dma("gpsimd", dbg["dbg_x1"][k * 128:(k + 1) * 128, :], xsl(k, 0, TF))
        B.barrier()
    emit_attn()
    if debug:
        for k in range(KD):
            B.dma("gpsimd", dbg["dbg_x2"][k * 128:(k + 1) * 128, :], xsl(k, 0, TF))
        B.barrier()
    emit_ffn(2)

    for k in range(KD):
        B.dma("sync", yout[k * 128:(k + 1) * 128, :], xsl(k, 1, TM))
    B.wait_dmas("sync", "sync")

    with nc.Block() as block:
        @block.sync
        def _(e):
            B.run("sync", e)

        @block.tensor
        def _(e):
            B.run("tensor", e)

        @block.vector
        def _(e):
            B.run("vector", e)

        @block.scalar
        def _(e):
            B.run("scalar", e)

        @block.gpsimd
        def _(e):
            B.run("gpsimd", e)

    es.close()
    return nc


def make_in_maps(inputs):
    x = np.ascontiguousarray(np.asarray(inputs["L0_attn_x"], dtype=np.float32))
    P = np.asarray(inputs["L0_ffn_down_reorder_index"]).astype(np.int64)
    wg = np.asarray(inputs["L0_ffn_w_gate"], dtype=np.float32)
    wu = np.asarray(inputs["L0_ffn_w_up"], dtype=np.float32)
    wd = np.asarray(inputs["L0_ffn_w_down"], dtype=np.float32)
    wk = np.asarray(inputs["L0_attn_Wk"], dtype=np.float32)
    wv = np.asarray(inputs["L0_attn_Wv"], dtype=np.float32)
    wr = np.asarray(inputs["L0_attn_Wr"], dtype=np.float32)
    td = np.asarray(inputs["L0_attn_time_decay"], dtype=np.float32)

    wg_nat = np.ascontiguousarray(wg[P, :])
    wu_nat = np.ascontiguousarray(wu[P, :])

    def til(wmat, nm):
        no, ni = wmat.shape
        a = wmat.reshape(no // 128, 128, ni // 128, 128)
        return np.ascontiguousarray(a.transpose(0, 3, 2, 1).reshape(no // 128, 128, ni))

    wgT = til(wg_nat, "g")
    wuT = til(wu_nat, "u")
    wdT = til(wd, "d")
    wkT = til(wk, "k")
    wvT = til(wv, "v")
    wrT = til(wr, "r")
    tdt = np.ascontiguousarray(td.reshape(KD, 128).T)
    iota = np.arange(1, TM + 1, dtype=np.float32).reshape(1, TM)
    ident = np.eye(128, dtype=np.float32)
    onesbf = np.ones((128, 1), dtype=ml_dtypes.bfloat16)
    onesf = np.ones((128, 128), dtype=np.float32)

    in_maps = []
    for c in range(8):
        b, h = c // 2, c % 2
        if h == 0:
            xs = np.concatenate([np.zeros((1, D), np.float32), x[b, 0:TM]], axis=0)
        else:
            xs = x[b, TM - 1:2 * TM]
        in_maps.append({
            "xin": np.ascontiguousarray(xs.T),
            "wkT": wkT, "wvT": wvT, "wrT": wrT,
            "wgT": wgT, "wuT": wuT, "wdT": wdT, "tdt": tdt,
            "wg_nat": wg_nat, "wu_nat": wu_nat, "wd_nat": wd,
            "iota": iota, "identity": ident, "id224": ident * 224.0,
            "onesbf": onesbf, "onesf": onesf,
            "msend": np.full((128, 1), 1.0 - h, np.float32),
            "mrecv": np.full((128, 1), float(h), np.float32),
        })
    return in_maps


def kernel(**inputs):
    nc = build_program(debug=False)
    in_maps = make_in_maps(inputs)
    res = run_bass_kernel_spmd(nc, in_maps, core_ids=list(range(8)))
    out = np.empty((4, 2048, D), dtype=np.float32)
    for c in range(8):
        b, h = c // 2, c % 2
        out[b, h * TM:(h + 1) * TM, :] = res.results[c]["yout"].T
    return out
